# version 08 (final)
# baseline (speedup 1.0000x reference)
import os
import numpy as np
import ml_dtypes
from contextlib import ExitStack

_DBG = os.environ.get("KDBG") == "1"

import concourse.bass as bass
import concourse.mybir as mybir
from concourse.bass_utils import run_bass_kernel_spmd

F32 = mybir.dt.float32
BF16 = mybir.dt.bfloat16
FP8 = mybir.dt.float8e4
I32 = mybir.dt.int32
AX = mybir.AxisListType.X
OP = mybir.AluOpType
ACT = mybir.ActivationFunctionType

B, T, D, H = 4, 4096, 2048, 16
NT = 2048
NQT = 16
TG = 512
NG = 4
NC = 16
SCALE = 1.0 / float(np.sqrt(128.0))
EPS = 1e-5
DQP = 768
DKVP = 1152
HID = 5632
NHC = 44


def _bf(x):
    return np.asarray(x).astype(ml_dtypes.bfloat16)


def build_rotmats():
    inv = (1.0 / 10000.0) ** (np.arange(0, 64, 2, dtype=np.float32) / np.float32(64))
    R = np.zeros((H, 128, 128), np.float32)
    for h in range(H):
        R[h, :64, :64] = np.eye(64, dtype=np.float32)
        c = np.cos(np.float32(h) * inv)
        s = np.sin(np.float32(h) * inv)
        for j in range(32):
            R[h, 64 + j, 64 + j] = c[j]
            R[h, 64 + j, 96 + j] = s[j]
            R[h, 96 + j, 64 + j] = -s[j]
            R[h, 96 + j, 96 + j] = c[j]
    return _bf(np.ascontiguousarray(R.transpose(0, 2, 1)))


def build_masks(parity):
    k = np.arange(128)[:, None]
    q = np.arange(128)[None, :]
    tri = (k <= q).astype(np.float32)
    ones = np.ones((128, 128), np.float32)
    zero = np.zeros((128, 128), np.float32)
    m = np.stack([tri, zero]) if parity == 0 else np.stack([ones, tri])
    return _bf(m)


def stage_inputs(z):
    def padw(a, rows, cols):
        out = np.zeros((rows, cols), a.dtype)
        out[:a.shape[0], :a.shape[1]] = a
        return out

    def padv(a, n):
        out = np.zeros((n,), a.dtype)
        out[:a.shape[0]] = a
        return out

    com = {}
    com["w0f"] = np.ascontiguousarray(np.asarray(z["L0_ffn_weight"]).T).astype(np.float32)
    com["wdqT"] = padw(np.ascontiguousarray(z["L0_attn_W_d_q"].T), D, DQP)
    com["bdq"] = padv(z["L0_attn_b_d_q"], DQP)
    com["qng"] = padv(z["L0_attn_q_norm_g"], DQP)
    com["qnb"] = padv(z["L0_attn_q_norm_b"], DQP)
    com["wuqT"] = padw(np.ascontiguousarray(z["L0_attn_W_u_q"].T), DQP, D)
    com["buq"] = np.ascontiguousarray(z["L0_attn_b_u_q"])
    com["wdkvT"] = padw(np.ascontiguousarray(z["L0_attn_W_d_kv"].T), D, DKVP)
    com["bdkv"] = padv(z["L0_attn_b_d_kv"], DKVP)
    com["kvg"] = np.ascontiguousarray(z["L0_attn_kv_norm_g"])
    com["kvb"] = np.ascontiguousarray(z["L0_attn_kv_norm_b"])
    wukv = np.asarray(z["L0_attn_W_u_kv"])
    bukv = np.asarray(z["L0_attn_b_u_kv"])
    kn_rows = np.concatenate([np.arange(192 * h, 192 * h + 64) for h in range(H)])
    v_rows = np.concatenate([np.arange(192 * h + 64, 192 * (h + 1)) for h in range(H)])
    com["wknT"] = np.ascontiguousarray(wukv[kn_rows].T)
    com["bkn"] = np.ascontiguousarray(bukv[kn_rows])
    com["wvT0"] = np.ascontiguousarray(wukv[v_rows].T)
    com["bv0"] = np.ascontiguousarray(bukv[v_rows]).reshape(1, D)
    com["woT0"] = np.ascontiguousarray(z["L0_attn_W_o"].T)
    com["bo0"] = np.ascontiguousarray(z["L0_attn_b_o"])
    com["wq1"] = np.ascontiguousarray(z["L1_attn_Wq"])
    com["bq1"] = np.ascontiguousarray(np.asarray(z["L1_attn_bq"]).reshape(-1))
    com["wk1"] = np.ascontiguousarray(z["L1_attn_Wk"])
    com["bk1"] = np.ascontiguousarray(np.asarray(z["L1_attn_bk"]).reshape(-1))
    com["wv1r"] = np.ascontiguousarray(
        np.asarray(z["L1_attn_Wv"]).transpose(1, 0, 2).reshape(D, D))
    com["bv1"] = np.ascontiguousarray(np.asarray(z["L1_attn_bv"]).reshape(-1)).reshape(1, D)
    com["wo1"] = np.ascontiguousarray(z["L1_attn_Wo"])
    com["bo1"] = np.ascontiguousarray(z["L1_attn_bo"])
    com["wgT"] = np.ascontiguousarray(z["L1_ffn_wq_gate"].T)
    com["sg"] = np.ascontiguousarray(z["L1_ffn_s_gate"])
    com["wuT"] = np.ascontiguousarray(z["L1_ffn_wq_up"].T)
    com["su"] = np.ascontiguousarray(z["L1_ffn_s_up"])
    com["wdT"] = np.ascontiguousarray(z["L1_ffn_wq_down"].T)
    com["sd"] = np.ascontiguousarray(z["L1_ffn_s_down"])
    com["rotq"] = build_rotmats().astype(np.float32)

    x = np.asarray(z["L0_attn_x"])
    in_maps = []
    for c in range(8):
        b, p = c // 2, c % 2
        toks = np.concatenate([np.arange(128 * (2 * i + p), 128 * (2 * i + p) + 128)
                               for i in range(NQT)])
        m = dict(com)
        m["xT"] = np.ascontiguousarray(x[b, toks].T)
        m["masks"] = build_masks(p).astype(np.float32)
        in_maps.append(m)
    return in_maps


def build_program():
    nc = bass.Bass("TRN2", target_bir_lowering=False, debug=False)
    ap = {}

    def din(name, shape, dt):
        ap[name] = nc.dram_tensor(name, list(shape), dt, kind="ExternalInput").ap()

    din("w0f", (D, D), F32)
    din("xT", (D, NT), F32)
    din("wdqT", (D, DQP), F32); din("bdq", (DQP,), F32)
    din("qng", (DQP,), F32); din("qnb", (DQP,), F32)
    din("wuqT", (DQP, D), F32); din("buq", (D,), F32)
    din("wdkvT", (D, DKVP), F32); din("bdkv", (DKVP,), F32)
    din("kvg", (1024,), F32); din("kvb", (1024,), F32)
    din("wknT", (1024, 1024), F32); din("bkn", (1024,), F32)
    din("wvT0", (1024, D), F32); din("bv0", (1, D), F32)
    din("woT0", (D, D), F32); din("bo0", (D,), F32)
    din("wq1", (H, D, 128), F32); din("bq1", (D,), F32)
    din("wk1", (H, D, 128), F32); din("bk1", (D,), F32)
    din("wv1r", (D, D), F32); din("bv1", (1, D), F32)
    din("wo1", (D, D), F32); din("bo1", (D,), F32)
    din("wgT", (D, HID), I32); din("sg", (HID,), F32)
    din("wuT", (D, HID), I32); din("su", (HID,), F32)
    din("wdT", (HID, D), I32); din("sd", (D,), F32)
    din("rotq", (H, 128, 128), F32)
    din("masks", (2, 128, 128), F32)

    xout = nc.dram_tensor("xout", [D, NT], F32, kind="ExternalOutput").ap()
    dk = dict(kind="ExternalOutput") if _DBG else {}
    xbuf = [nc.dram_tensor(f"xbuf{i}", [D, NT], F32, **dk).ap() for i in range(2)]
    QT = nc.dram_tensor("QT", [D, NT], BF16, **dk).ap()
    Kloc = nc.dram_tensor("Kloc", [D, NT], BF16).ap()
    Vloc = nc.dram_tensor("Vloc", [NT, D], BF16).ap()
    Kall = [nc.dram_tensor(f"Kall{i}", [2 * D, NT], BF16).ap() for i in range(2)]
    Vall = [nc.dram_tensor(f"Vall{i}", [2 * NT, D], BF16).ap() for i in range(2)]
    YT = nc.dram_tensor("YT", [D, NT], BF16, **dk).ap()
    dbgT = nc.dram_tensor("dbg", [128, 32 * 512], F32, kind="ExternalOutput").ap() if _DBG else None
    dbg2T = nc.dram_tensor("dbg2", [128, 256], F32, kind="ExternalOutput").ap() if _DBG else None
    ccm_i = nc.dram_tensor("ccm_i", [128, 128], F32).ap()
    ccm_o = nc.dram_tensor("ccm_o", [128, 128], F32).ap()
    w0Ti = nc.dram_tensor("w0Ti", [D, D], F32).ap()
    wq1i = nc.dram_tensor("wq1i", [H, D, 128], F32).ap()
    wk1i = nc.dram_tensor("wk1i", [H, D, 128], F32).ap()
    wv1ri = nc.dram_tensor("wv1ri", [D, D], F32).ap()
    wo1i = nc.dram_tensor("wo1i", [D, D], F32).ap()
    wgB = nc.dram_tensor("wgB", [D, HID], BF16).ap()
    wuB = nc.dram_tensor("wuB", [D, HID], BF16).ap()
    wdB = nc.dram_tensor("wdB", [HID, D], BF16).ap()

    st = ExitStack()
    ent = st.enter_context

    def sb(name, shape, dt):
        return ent(nc.sbuf_tensor(name, list(shape), dt))

    bias = sb("bias", [128, 160], F32)
    bcol = {}
    _c = [0]
    for nm, ncols in [("bdq", 6), ("qng", 6), ("qnb", 6), ("buq", 16),
                      ("bdkv", 9), ("kvg", 8), ("kvb", 8), ("bkn", 8),
                      ("bo0", 16), ("bq1", 16), ("bk1", 16), ("bo1", 16),
                      ("sd", 16)]:
        bcol[nm] = _c[0]
        _c[0] += ncols
    assert _c[0] <= 160, _c[0]
    bias2 = sb("bias2", [128, 64], F32)
    bcol2 = {"bo0": 0, "bq1": 16, "bk1": 32, "bo1": 48}
    sgc = sb("sgc", [128, 44], F32)
    suc = sb("suc", [128, 44], F32)
    ones128 = sb("ones128", [128, 1], F32)
    ones128b = sb("ones128b", [128, 1], BF16)
    ones1 = sb("ones1", [1, 128], F32)
    brow = sb("brow", [1, 512], F32)
    masksb = sb("masksb", [128, 256], BF16)
    rotsb = sb("rotsb", [128, H * 128], BF16)
    sca = sb("sca", [128, 4], F32)
    srow = sb("srow", [1, 16], F32)
    arow = sb("arow", [1, 512], F32)
    wrow = sb("wrow", [1, 512], F32)
    bc2 = sb("bc2", [128, 1024], F32)
    amaxp = sb("amaxp", [128, 2], F32)
    amaxw = sb("amaxw", [128, 1], F32)
    amax_cols = sb("amaxc", [128, 8], F32)
    invr = sb("invr", [1, TG], F32)
    invb = sb("invb", [128, TG], F32)
    mur = sb("mur", [1, TG], F32)
    ivr = sb("ivr", [1, TG], F32)
    mub = sb("mub", [128, TG], F32)
    ivb = sb("ivb", [128, TG], F32)
    evq = sb("evq", [128, TG], BF16)
    yev = sb("yev", [128, TG], BF16)
    sqc = sb("sqc", [128, TG], F32)
    stg = sb("stg", [128, TG], F32)
    gg = sb("gg", [128, TG], F32)
    Pt = [sb(f"Pt{i}", [128, TG], BF16) for i in range(2)]

    xf = sb("xf", [128, NC * TG], F32)
    xb = sb("xb", [128, NC * TG], BF16)
    x8 = sb("x8", [128, NC * TG], FP8)
    _OVL = 183264
    def sba(name, shape, dt, off):
        return nc.alloc_sbuf_tensor_at(name, list(shape), dt, offset=off)
    cq = sba("cq", [128, 6 * TG], F32, _OVL)
    ckv = sba("ckv", [128, 9 * TG], F32, _OVL + 12288)
    cqb = sba("cqb", [128, 6 * TG], BF16, _OVL + 30720)
    ckvb = sba("ckvb", [128, 9 * TG], BF16, _OVL + 36864)
    wA = sb("wA", [128, 8192], BF16)
    wB = sb("wB", [128, 8192], BF16)
    wC = sb("wC", [128, NHC * 128], BF16)
    w8a = sb("w8a", [128, 16 * 128], FP8)
    w8b = sb("w8b", [128, 16 * 128], FP8)
    wi32 = sb("wi32", [128, 16 * 128], I32)
    hh = sba("hh", [128, NHC * TG], BF16, _OVL)
    Ks = [sba(f"Ks{i}", [128, 2 * NT], BF16, _OVL + i * 8192) for i in range(2)]
    Vs = [sba(f"Vs{i}", [128, 2 * NT], BF16, _OVL + 16384 + i * 8192)
          for i in range(2)]
    Qs = [sba(f"Qs{i}", [128, NT], BF16, _OVL + 32768 + i * 4096)
          for i in range(2)]

    psA = ent(nc.psum_tensor("psA", [128, 512], F32))
    psB = ent(nc.psum_tensor("psB", [128, 512], F32))
    psC = ent(nc.psum_tensor("psC", [128, 512], F32))
    psD = ent(nc.psum_tensor("psD", [128, 512], F32))
    psS0 = ent(nc.psum_tensor("psS0", [128, 512], F32))
    psS1 = ent(nc.psum_tensor("psS1", [128, 512], F32))
    psU0 = ent(nc.psum_tensor("psU0", [128, 512], F32))
    psU1 = ent(nc.psum_tensor("psU1", [128, 512], F32))
    PS = {"pA": psA, "pB": psB, "pC": psC, "pD": psD,
          "pS0": psS0, "pS1": psS1, "pU0": psU0, "pU1": psU1}

    def cb_src(src2d, col0, ncols128, rowchunks):
        return src2d[:, col0:col0 + ncols128 * 128].rearrange(
            "(c p) q -> p c q", p=128)

    def cb_dst(buf, rowchunks, qcols):
        return buf[:, 0:rowchunks * qcols].rearrange(
            "p (c q) -> p c q", q=qcols)

    sems = {}
    cnt = {}
    for nm in ["dS", "dA", "dW", "dG", "vv", "ss", "tt", "gp", "cc", "bar"]:
        sems[nm] = ent(nc.semaphore(nm))
        cnt[nm] = 0

    ent(nc.allow_non_contiguous_dma(reason="small bias/column loads"))
    block = ent(nc.Block())
    prog = {e: [] for e in ["sync", "vector", "scalar", "tensor", "gpsimd"]}

    def em(eng, fn):
        prog[eng].append(fn)

    def inc(nm, by=1):
        cnt[nm] += by
        return cnt[nm]

    QSEM = {"sync": "dS", "scalar": "dA", "gpsimd": "dG"}

    def dma(eng, _ignored, out, in_):
        snm = QSEM[eng]
        v = inc(snm, 16)
        em(eng, lambda e, o=out, i=in_, s=sems[snm]:
           e.dma_start(out=o, in_=i).then_inc(s, 16))
        return (snm, v)

    def wait(eng, snm, val=None):
        if isinstance(snm, tuple):
            snm, val = snm
        if val > 0:
            em(eng, lambda e, s=sems[snm], v=val: e.wait_ge(s, v))

    def barrier():
        target = cnt["bar"] + 5
        for e in prog:
            em(e, lambda eng, s=sems["bar"]: eng.sem_inc(s, 1))
        cnt["bar"] = target
        for e in prog:
            wait(e, "bar", target)

    _NOG = os.environ.get("KNOGUARD") == "1"

    def guard(eng):
        if _NOG:
            return
        for snm in ["vv", "ss", "tt"]:
            wait(eng, snm, cnt[snm])

    _dbg_slot = [0]

    def dbg_dump(srcap, wait_sems=()):
        if not _DBG or _dbg_slot[0] >= 32:
            return
        sl = _dbg_slot[0]
        _dbg_slot[0] += 1
        for wdep in wait_sems:
            if isinstance(wdep, tuple) and isinstance(wdep[0], str) and wdep[0] in sems:
                wait("gpsimd", wdep[0], wdep[1])
            else:
                wait("gpsimd", wdep)
        guard("gpsimd")
        r, c = srcap.shape[0], int(np.prod(srcap.shape[1:]))
        dma("gpsimd", "dS", dbgT[0:r, sl * 512:sl * 512 + c], srcap)

    PSFREE = {nm: ("ss", 0) for nm in PS}
    SBFREE = {}

    def sbdep(name):
        return [SBFREE[name]] if name in SBFREE else []

    def mm_group(pnm, lhsT_list, rhs_list, deps=(), width=512, col0=0,
                 row0=0, rows=128):
        ps = PS[pnm]
        for dep in deps:
            wait("tensor", dep[0], dep[1])
        s, v = PSFREE[pnm]
        wait("tensor", s, v)
        tv = inc("tt")
        outap = ps[row0:row0 + rows, col0:col0 + width]

        def go(e, ls=tuple(lhsT_list), rs=tuple(rhs_list), o=outap):
            m = None
            for i, (l, r) in enumerate(zip(ls, rs)):
                m = e.matmul(o, l, r, start=(i == 0), stop=(i == len(ls) - 1))
            m.then_inc(sems["tt"], 1)
        em("tensor", go)
        return tv

    def evac(pnm, out_ap, tv, func=ACT.Copy, biasap=0.0, scaleap=1.0,
             width=512, deps=()):
        if func == ACT.Copy and not (isinstance(biasap, float)
                                     and isinstance(scaleap, float)):
            func = ACT.Identity
        wait("scalar", "tt", tv)
        for dep in deps:
            wait("scalar", dep[0], dep[1])
        sv = inc("ss")
        em("scalar", lambda e, o=out_ap, i=PS[pnm][:, 0:width], f=func,
           b=biasap, s=scaleap:
           e.activation(out=o, in_=i, func=f, bias=b, scale=s)
           .then_inc(sems["ss"], 1))
        PSFREE[pnm] = ("ss", sv)
        return sv

    for nm in ["bdq", "qng", "qnb", "bdkv", "kvg", "kvb", "bkn", "bo0",
               "bq1", "bk1", "bo1", "sd", "buq"]:
        ncols = {"bdq": 6, "qng": 6, "qnb": 6, "bdkv": 9, "kvg": 8, "kvb": 8,
                 "bkn": 8, "bo0": 16, "bq1": 16, "bk1": 16, "bo1": 16,
                 "sd": 16, "buq": 16}[nm]
        dma("sync", "dS", bias[:, bcol[nm]:bcol[nm] + ncols],
            ap[nm].rearrange("(c p) -> p c", p=128))
    dma("sync", "dS", sgc[:, :], ap["sg"].rearrange("(c p) -> p c", p=128))
    dma("sync", "dS", suc[:, :], ap["su"].rearrange("(c p) -> p c", p=128))
    for m in range(2):
        dma("gpsimd", "dS", masksb[:, m * 128:(m + 1) * 128], ap["masks"][m])
    for h_ in range(H):
        PRO = dma("gpsimd", "dS", rotsb[:, h_ * 128:(h_ + 1) * 128],
                  ap["rotq"][h_])
    CPV = ("dS", 0)
    vc0 = inc("vv", 3)
    em("vector", lambda e: (e.memset(ones128[:, :], 1.0),
                            e.memset(ones128b[:, :], 1.0),
                            e.memset(arow[:, :], 0.0),
                            e.memset(wrow[:, :], 0.0),
                            e.memset(ones1[:, :], 1.0).then_inc(sems["vv"], 3)))
    ONES = vc0

    def emit_dequant(src, dst, nblk, colmode, eng):
        snm = {"vector": "vv", "scalar": "ss", "gpsimd": "gp"}[eng]
        prev_out = [("dS", 0)]
        for j in range(nblk):
            if colmode:
                sap = src[:, j * 128:(j + 1) * 128].rearrange(
                    "(c p) q -> p c q", p=128)
                dap = dst[:, j * 128:(j + 1) * 128].rearrange(
                    "(c p) q -> p c q", p=128)
                n = 16 * 128
            else:
                sap = src[j * 128:(j + 1) * 128, :]
                dap = dst[j * 128:(j + 1) * 128, :]
                n = 2048
            guard("sync")
            v = dma("sync", "dW",
                    wi32[:, 0:n] if not colmode else cb_dst(wi32, 16, 128), sap)
            wait(eng, v)
            wait(eng, prev_out[0])
            sv = inc(snm)
            if eng == "scalar":
                em(eng, lambda e, i=wi32, nn=n:
                   e.activation(out=wA[:, 0:nn] if nn <= 2048 else None,
                                in_=i[:, 0:nn], func=ACT.Copy, bias=-128.0)
                   .then_inc(sems["ss"], 1))
            else:
                em(eng, lambda e, i=wi32, nn=n, s=snm:
                   e.tensor_scalar(out=wA[:, 0:nn], in0=i[:, 0:nn],
                                   scalar1=-128.0, scalar2=None, op0=OP.add)
                   .then_inc(sems[s], 1))
            wait("sync", snm, sv)
            prev_out[0] = dma("sync", "dW", dap,
                              wA[:, 0:n] if not colmode else cb_dst(wA, 16, 128))

    emit_dequant(ap["wgT"], wgB, NHC, True, "vector")
    emit_dequant(ap["wuT"], wuB, NHC, True, "vector")
    emit_dequant(ap["wdT"], wdB, NHC, False, "vector")
    barrier()

    def load_wblock_bf16(dst_sb, ncols128, src2d, col0, rowchunks, fbuf=None):
        src = cb_src(src2d, col0, ncols128, rowchunks)
        guard("gpsimd")
        v = dma("gpsimd", "dW", cb_dst(dst_sb, rowchunks, ncols128 * 128), src)
        return v

    def emit_xload(xsrc, tg, cast=False):
        src = xsrc.rearrange("(c p) t -> p c t", p=128)[
            :, :, tg * TG:(tg + 1) * TG]
        guard("sync")
        v = dma("sync", "dS", cb_dst(xf, NC, TG), src)
        cv = 0
        if cast:
            wait("vector", v)
            cv = inc("vv")
            em("vector", lambda e:
               e.tensor_copy(out=xb[:, 0:NC * TG], in_=xf[:, 0:NC * TG])
               .then_inc(sems["vv"], 1))
        return v, cv

    def ln_phase(nchunks, last_rows, denom, gname=None, gbname=None):
        raise RuntimeError("use ln_on")

    def ln_on(r, nchunks, last_rows, denom, gname=None, gbname=None,
              out_bf=None):
        lh, rh = [], []
        for c in range(nchunks):
            rows = last_rows if c == nchunks - 1 else 128
            lh.append(ones128[0:rows, 0:1])
            rh.append(r[0:rows, c * TG:c * TG + TG])
        tv0 = mm_group("pU0", lh, rh, deps=[("vv", ONES)], rows=1)
        sqdeps = []
        for c in range(nchunks):
            rows = last_rows if c == nchunks - 1 else 128
            sv = inc("ss")
            em("scalar", lambda e, cc=c:
               e.activation(out=sqc[:, :], in_=r[:, cc * TG:cc * TG + TG],
                            func=ACT.Square).then_inc(sems["ss"], 1))
            s_, v_ = PSFREE["pU1"]
            wait("tensor", s_, v_)
            wait("tensor", "ss", sv)
            tvx = inc("tt")
            em("tensor", lambda e, rows_=rows, first=(c == 0),
               last=(c == nchunks - 1):
               e.matmul(psU1[0:1, 0:TG], ones128[0:rows_, 0:1],
                        sqc[0:rows_, :], start=first, stop=last)
               .then_inc(sems["tt"], 1))
            if c == nchunks - 1:
                tv1 = tvx
            wait("scalar", "tt", tvx)
        wait("vector", "tt", tv1)
        wait("vector", "tt", tv0)
        dn = float(denom)
        vbase = cnt["vv"]
        cnt["vv"] = vbase + 5

        def stats(e):
            k = [vbase - 5 + 5]
            k = [vbase]

            def stp(i):
                k[0] += 1
                i.then_inc(sems["vv"], 1)
                e.wait_ge(sems["vv"], k[0])
            stp(e.tensor_scalar(out=mur[0:1, :], in0=psU0[0:1, 0:TG],
                                scalar1=1.0 / dn, scalar2=None, op0=OP.mult))
            stp(e.tensor_scalar(out=ivr[0:1, :], in0=psU1[0:1, 0:TG],
                                scalar1=1.0 / dn, scalar2=None, op0=OP.mult))
            stp(e.tensor_tensor(out=invr[0:1, :], in0=mur[0:1, :],
                                in1=mur[0:1, :], op=OP.mult))
            stp(e.tensor_tensor(out=ivr[0:1, :], in0=ivr[0:1, :],
                                in1=invr[0:1, :], op=OP.subtract))
            stp(e.tensor_scalar(out=ivr[0:1, :], in0=ivr[0:1, :], scalar1=EPS,
                                scalar2=None, op0=OP.add))
            e.reciprocal(out=ivr[0:1, :], in_=ivr[0:1, :]).then_inc(sems["vv"], 1)
        vv0 = inc("vv")
        em("vector", stats)
        PSFREE["pU0"] = ("vv", vv0)
        PSFREE["pU1"] = ("vv", vv0)
        wait("scalar", "vv", vv0)
        sv2 = inc("ss")
        em("scalar", lambda e: e.activation(out=ivr[0:1, :], in_=ivr[0:1, :],
                                            func=ACT.Sqrt).then_inc(sems["ss"], 1))
        tb0 = mm_group("pS0", [ones1[0:1, :]], [mur[0:1, :]],
                       deps=[("vv", vv0)])
        tb1 = mm_group("pS1", [ones1[0:1, :]], [ivr[0:1, :]],
                       deps=[("ss", sv2)])
        evac("pS0", mub[:, :], tb0, deps=sbdep("mub"))
        sv4 = evac("pS1", ivb[:, :], tb1, deps=sbdep("ivb"))
        wait("vector", "ss", sv4)

        nops = nchunks * ((2 if out_bf is not None else 2) if gname is None
                          else 3) if gname is None else nchunks * 3
        nops = nchunks * ((3 if out_bf is not None else 2)
                          if gname is None else 3)
        vb2 = cnt["vv"]
        cnt["vv"] = vb2 + nops

        def norm(e, n=nchunks):
            k = [vb2]

            def stp(i):
                k[0] += 1
                i.then_inc(sems["vv"], 1)
                e.wait_ge(sems["vv"], k[0])
            for c in range(n):
                a = r[:, c * TG:c * TG + TG]
                stp(e.tensor_tensor(out=stg[:, :], in0=a, in1=mub[:, :],
                                    op=OP.subtract))
                if gname is None:
                    stp(e.tensor_tensor(out=a, in0=stg[:, :], in1=ivb[:, :],
                                        op=OP.mult))
                    if out_bf is not None:
                        stp(e.tensor_copy(out=out_bf[:, c * TG:c * TG + TG],
                                          in_=a))
                else:
                    stp(e.tensor_tensor(out=stg[:, :], in0=stg[:, :],
                                        in1=ivb[:, :], op=OP.mult))
                    g = bias[:, bcol[gname] + c:bcol[gname] + c + 1]
                    bb = bias[:, bcol[gbname] + c:bcol[gbname] + c + 1]
                    tgt = a if out_bf is None else out_bf[:, c * TG:c * TG + TG]
                    stp(e.tensor_scalar(out=tgt, in0=stg[:, :], scalar1=g,
                                        scalar2=bb, op0=OP.mult, op1=OP.add))
        em("vector", norm)
        vout = cnt["vv"]
        SBFREE["mub"] = ("vv", vout)
        SBFREE["ivb"] = ("vv", vout)
        return vout

    def qkv_l0(xsrc):
        wv0 = 0
        for tg in range(NG):
            xv, xbv = emit_xload(xsrc, tg, cast=True)
            if tg == 0:
                dbg_dump(xf[:, 0:512], [xv])
                dbg_dump(xb[:, 0:512], [("vv", xbv)])
            for (wname, nocs, cdst, bname) in [("wdqT", 6, cq, "bdq"),
                                               ("wdkvT", 9, ckv, "bdkv")]:
                for oc in range(nocs):
                    wbuf = wA if oc % 2 == 0 else wB
                    wv = load_wblock_bf16(wbuf, 1, ap[wname], oc * 128, 16)
                    pnm = ["pA", "pB", "pC", "pD"][oc % 4]
                    lh = [wbuf[:, k * 128:(k + 1) * 128] for k in range(16)]
                    rh = [xb[:, k * TG:(k + 1) * TG] for k in range(16)]
                    tv = mm_group(pnm, lh, rh, deps=[wv, ("vv", xbv)])
                    sv_ = evac(pnm, cdst[:, oc * TG:(oc + 1) * TG], tv,
                               biasap=bias[:, bcol[bname] + oc:bcol[bname] + oc + 1])
                    if tg == 0 and oc == 0:
                        dbg_dump(wbuf[:, 0:512], [wv])
                        dbg_dump(cdst[:, 0:512], [("ss", sv_)])
            lnv_ = ln_on(cq, 6, 42, 682, gname="qng", gbname="qnb", out_bf=cqb)
            if tg == 0:
                dbg_dump(cqb[:, 0:512], [("vv", lnv_)])
            ln_on(ckv, 8, 128, 1024, gname="kvg", gbname="kvb", out_bf=ckvb)
            crv = inc("vv")
            em("vector", lambda e:
               e.tensor_copy(out=ckvb[:, 8 * TG:9 * TG],
                             in_=ckv[:, 8 * TG:9 * TG]).then_inc(sems["vv"], 1))
            for h in range(H):
                wbuf = wA if h % 2 == 0 else wB
                wv = load_wblock_bf16(wbuf, 1, ap["wuqT"], h * 128, 6)
                pnm = ["pA", "pB"][h % 2]
                lh = [wbuf[:, k * 128:(k + 1) * 128] for k in range(6)]
                rh = [cqb[:, k * TG:(k + 1) * TG] for k in range(6)]
                tv = mm_group(pnm, lh, rh, deps=[wv, ("vv", crv)])
                sv = evac(pnm, evq[:, :], tv,
                          biasap=bias[:, bcol["buq"] + h:bcol["buq"] + h + 1],
                          deps=sbdep("evq"))
                if tg == 0 and h == 0:
                    dbg_dump(evq[:, :], [("ss", sv)])
                    dbg_dump(rotsb[:, 0:512], [PRO])
                pnm2 = ["pC", "pD"][h % 2]
                tv2 = mm_group(pnm2, [rotsb[:, h * 128:(h + 1) * 128]],
                               [evq[:, :]], deps=[("ss", sv), PRO])
                SBFREE["evq"] = ("tt", tv2)
                sv2 = evac(pnm2, yev[:, :], tv2, deps=sbdep("yev"))
                if tg == 0 and h == 0:
                    dbg_dump(yev[:, :], [("ss", sv2)])
                wait("sync", "ss", sv2)
                dv = dma("sync", "dS",
                    QT[h * 128:(h + 1) * 128, tg * TG:(tg + 1) * TG], yev[:, :])
                SBFREE["yev"] = dv
            for oc in range(8):
                wbuf = wA if oc % 2 == 0 else wB
                wv = load_wblock_bf16(wbuf, 1, ap["wknT"], oc * 128, 8)
                pnm = ["pA", "pB"][oc % 2]
                lh = [wbuf[:, k * 128:(k + 1) * 128] for k in range(8)]
                rh = [ckvb[:, k * TG:(k + 1) * TG] for k in range(8)]
                tv = mm_group(pnm, lh, rh, deps=[wv, ("vv", crv)])
                sv = evac(pnm, evq[:, :], tv,
                          biasap=bias[:, bcol["bkn"] + oc:bcol["bkn"] + oc + 1],
                          deps=sbdep("evq"))
                wait("sync", "ss", sv)
                dv = 0
                for half in range(2):
                    h = 2 * oc + half
                    dv = dma("sync", "dS",
                        Kloc[h * 128:h * 128 + 64, tg * TG:(tg + 1) * TG],
                        evq[half * 64:half * 64 + 64, :])
                SBFREE["evq"] = dv
            wait("sync", "vv", crv)
            for h in range(H):
                dma("sync", "dS",
                    Kloc[h * 128 + 64:(h + 1) * 128, tg * TG:(tg + 1) * TG],
                    ckvb[0:64, 8 * TG:9 * TG])
            for hg in range(4):
                wv = load_wblock_bf16(wA, 4, ap["wvT0"], hg * 512, 8)
                guard("sync")
                bv = dma("sync", "dW", brow[0:1, :],
                         ap["bv0"][0:1, hg * 512:(hg + 1) * 512])
                for tt in range(4):
                    pnm = ["pA", "pB"][tt % 2]
                    lh = [ckvb[:, k * TG + tt * 128:k * TG + tt * 128 + 128]
                          for k in range(8)] + [ones1[0:1, :]]
                    rh = [wA[:, k * 512:(k + 1) * 512] for k in range(8)] + \
                         [brow[0:1, :]]
                    tv = mm_group(pnm, lh, rh,
                                  deps=[wv, bv, ("vv", crv)])
                    sv = evac(pnm, yev[:, :], tv, deps=sbdep("yev"))
                    wait("sync", "ss", sv)
                    dv = dma("sync", "dS",
                        Vloc[tg * TG + tt * 128:tg * TG + tt * 128 + 128,
                             hg * 512:hg * 512 + 512], yev[:, :])
                    SBFREE["yev"] = dv

    def qkv_l1(xsrc):
        wait("gpsimd", CPV)
        wait("sync", CPV)
        for tg in range(NG):
            xv, xbv = emit_xload(xsrc, tg, cast=True)
            xbr = xb.ap().rearrange("p (c t) -> p c t", c=NC)
            for (wname, bname, dstd) in [("wq1", "bq1", QT), ("wk1", "bk1", Kloc)]:
                for h in range(H):
                    wbuf = wA if h % 2 == 0 else wB
                    guard("gpsimd")
                    wv = dma("gpsimd", "dW", cb_dst(wbuf, 16, 128),
                             ap[wname][h].rearrange("(c p) d -> p c d", p=128))
                    pnm = ["pA", "pB"][h % 2]
                    lh = [wbuf[:, k * 128:(k + 1) * 128] for k in range(16)]
                    rh = [xb[:, k * TG:(k + 1) * TG] for k in range(16)]
                    tv = mm_group(pnm, lh, rh, deps=[wv, ("vv", xbv)])
                    sv = evac(pnm, yev[:, :], tv,
                              biasap=bias[:, bcol[bname] + h:bcol[bname] + h + 1],
                              deps=sbdep("yev"))
                    wait("sync", "ss", sv)
                    dv = dma("sync", "dS",
                        dstd[h * 128:(h + 1) * 128, tg * TG:(tg + 1) * TG],
                        yev[:, :])
                    SBFREE["yev"] = dv
            for hg in range(4):
                wv = load_wblock_bf16(wA, 4, ap["wv1r"], hg * 512, 16)
                guard("sync")
                bv = dma("sync", "dW", brow[0:1, :],
                         ap["bv1"][0:1, hg * 512:(hg + 1) * 512])
                for tt in range(4):
                    pnm = ["pA", "pB"][tt % 2]
                    lh = [xb[:, k * TG + tt * 128:k * TG + tt * 128 + 128]
                          for k in range(16)] + [ones1[0:1, :]]
                    rh = [wA[:, k * 512:(k + 1) * 512] for k in range(16)] + \
                         [brow[0:1, :]]
                    tv = mm_group(pnm, lh, rh,
                                  deps=[wv, bv, ("vv", xbv)])
                    sv = evac(pnm, yev[:, :], tv, deps=sbdep("yev"))
                    wait("sync", "ss", sv)
                    dv = dma("sync", "dS",
                        Vloc[tg * TG + tt * 128:tg * TG + tt * 128 + 128,
                             hg * 512:hg * 512 + 512], yev[:, :])
                    SBFREE["yev"] = dv

    CCV = [0]

    def allgather_kv(layer):
        wait("gpsimd", "dS", cnt["dS"])
        CCV[0] += 2
        groups = [[0, 1], [2, 3], [4, 5], [6, 7]]
        em("gpsimd", lambda e, l=layer: (
            e.collective_compute("AllGather", OP.bypass, replica_groups=groups,
                                 ins=[Kloc[:, :]], outs=[Kall[l][:, :]])
            .then_inc(sems["cc"], 1),
            e.collective_compute("AllGather", OP.bypass, replica_groups=groups,
                                 ins=[Vloc[:, :]], outs=[Vall[l][:, :]])
            .then_inc(sems["cc"], 1)))

    def attention(layer):
        KA, VA = Kall[layer], Vall[layer]
        wait("scalar", "cc", CCV[0])
        PT_FREE = [0, 0]
        ldv = {}
        STRIP_FREE = [0, 0]
        for h in range(H + 1):
            if h < H:
                hb = h % 2
                wait("scalar", "tt", STRIP_FREE[hb])
                dma("scalar", "dW", Ks[hb][:, 0:NT], KA[h * 128:(h + 1) * 128, :])
                dma("scalar", "dW", Ks[hb][:, NT:2 * NT],
                    KA[D + h * 128:D + (h + 1) * 128, :])
                dma("scalar", "dW", cb_dst(Vs[hb], 16, 128),
                    VA[0:NT, h * 128:(h + 1) * 128].rearrange(
                        "(j p) d -> p j d", p=128))
                dma("scalar", "dW",
                    Vs[hb][:, NT:2 * NT].rearrange("p (j d) -> p j d", d=128),
                    VA[NT:2 * NT, h * 128:(h + 1) * 128].rearrange(
                        "(j p) d -> p j d", p=128))
                ldv[h] = dma("scalar", "dW", Qs[hb][:, :],
                             QT[h * 128:(h + 1) * 128, :])
            if h == 0:
                continue
            hc = h - 1
            hb = hc % 2
            if layer == 0 and hc == 0:
                dbg_dump(Ks[hb][:, 0:512], [ldv[hc]])
                dbg_dump(Vs[hb][:, 0:512], [ldv[hc]])
                dbg_dump(Qs[hb][:, 0:512], [ldv[hc]])
            wait("tensor", ldv[hc])
            for g in range(NG):
                ktmax = 8 * g + 7
                ynm = ["pA", "pB"][g % 2]
                unm = ["pU0", "pU1"][g % 2]
                first = True
                last_tvU = 0
                for kt in range(ktmax + 1):
                    i0 = max(4 * g, kt // 2)
                    co = (i0 - 4 * g) * 128
                    w = 512 - co
                    sslot = kt % 2
                    snm = ["pS0", "pS1"][sslot]
                    kcol = (kt % 2) * NT + (kt // 2) * 128
                    tvS = mm_group(snm, [Ks[hb][:, kcol:kcol + 128]],
                                   [Qs[hb][:, g * TG + co:(g + 1) * TG]],
                                   width=w)
                    wait("scalar", "tt", tvS)
                    wait("scalar", "tt", PT_FREE[sslot])
                    sv = inc("ss")
                    em("scalar", lambda e, o=Pt[sslot][:, 0:w],
                       i=PS[snm][:, 0:w]:
                       e.activation(out=o, in_=i, func=ACT.Exp, scale=SCALE)
                       .then_inc(sems["ss"], 1))
                    PSFREE[snm] = ("ss", sv)
                    if layer == 0 and hc == 0 and g == 0 and kt == 0:
                        dbg_dump(Pt[sslot][:, 0:512], [("ss", sv)])
                    dep = ("ss", sv)
                    if kt // 2 == i0 and kt // 2 >= 4 * g:
                        slot = kt & 1
                        wait("vector", "ss", sv)
                        mv = inc("vv")
                        em("vector", lambda e, o=Pt[sslot][:, 0:128],
                           m=masksb[:, slot * 128:slot * 128 + 128]:
                           e.tensor_tensor(out=o, in0=o, in1=m, op=OP.mult)
                           .then_inc(sems["vv"], 1))
                        dep = ("vv", mv)
                    vcol = (kt % 2) * NT + (kt // 2) * 128
                    wait("tensor", dep[0], dep[1])
                    if first:
                        for pn in (ynm, unm):
                            s_, v_ = PSFREE[pn]
                            wait("tensor", s_, v_)
                    tvY = inc("tt")
                    em("tensor", lambda e, l=Vs[hb][:, vcol:vcol + 128],
                       r=Pt[sslot][:, 0:w], o=PS[ynm][:, co:512],
                       st_=first, sp=(kt == ktmax):
                       e.matmul(o, l, r, start=st_, stop=sp)
                       .then_inc(sems["tt"], 1))
                    tvU = inc("tt")
                    em("tensor", lambda e, r=Pt[sslot][:, 0:w],
                       o=PS[unm][0:1, co:512], st_=first, sp=(kt == ktmax):
                       e.matmul(o, ones128b[:, 0:1], r, start=st_, stop=sp)
                       .then_inc(sems["tt"], 1))
                    PT_FREE[sslot] = tvU
                    last_tvU = tvU
                    first = False
                wait("vector", "tt", last_tvU)
                rv = inc("vv")
                em("vector", lambda e, i=PS[unm][0:1, 0:512]:
                   e.reciprocal(out=invr[0:1, :], in_=i).then_inc(sems["vv"], 1))
                PSFREE[unm] = ("vv", rv)
                tvb = mm_group("pC", [ones1[0:1, :]], [invr[0:1, :]],
                               deps=[("vv", rv)])
                sv = evac("pC", invb[:, :], tvb, deps=sbdep("invb"))
                if layer == 0 and hc == 0 and g == 0:
                    dbg_dump(invb[:, 0:512], [("ss", sv)])
                wait("vector", "ss", sv)
                for dnm, dvl in sbdep("yev"):
                    wait("vector", dnm, dvl)
                yv = inc("vv")
                em("vector", lambda e, i=PS[ynm][:, 0:512]:
                   e.scalar_tensor_tensor(out=yev[:, :], in0=i, scalar=1.0,
                                          in1=invb[:, :], op0=OP.mult,
                                          op1=OP.mult).then_inc(sems["vv"], 1))
                SBFREE["invb"] = ("vv", yv)
                PSFREE[ynm] = ("vv", yv)
                if layer == 0 and hc == 0 and g <= 1:
                    dbg_dump(yev[:, 0:512], [("vv", yv)])
                wait("sync", "vv", yv)
                dv = dma("sync", "dS",
                    YT[hc * 128:(hc + 1) * 128, g * TG:(g + 1) * TG], yev[:, :])
                SBFREE["yev"] = dv
            STRIP_FREE[hb] = cnt["tt"]

    def residual_ln_store(xdst, tg, sub_is_bf=None):
        ln_on(xf, NC, 128, D)
        wait("sync", "vv", cnt["vv"])
        dst = xdst.rearrange("(c p) t -> p c t", p=128)[
            :, :, tg * TG:(tg + 1) * TG]
        dma("sync", "dS", dst, cb_dst(xf, NC, TG))

    def outproj_residual(layer, wname, bname, xsrc, xdst, amax):
        wait("gpsimd", CPV)
        for tg in range(NG):
            ysrc = YT.rearrange("(c p) t -> p c t", p=128)[
                :, :, tg * TG:(tg + 1) * TG]
            guard("sync")
            yv = dma("sync", "dS", cb_dst(xb, NC, TG), ysrc)
            xv, _ = emit_xload(xsrc, tg)
            for oc in range(NC):
                wbuf = wA if oc % 2 == 0 else wB
                wv = load_wblock_bf16(wbuf, 1, ap[wname], oc * 128, 16)
                pnm = ["pA", "pB"][oc % 2]
                lh = [wbuf[:, k * 128:(k + 1) * 128] for k in range(16)]
                rh = [xb[:, k * TG:(k + 1) * TG] for k in range(16)]
                tv = mm_group(pnm, lh, rh, deps=[wv, yv])
                sv = evac(pnm, stg[:, :], tv,
                          biasap=bias[:, bcol[bname] + oc:bcol[bname] + oc + 1],
                          deps=sbdep("stg"))
                wait("vector", "ss", sv)
                wait("vector", xv)
                av = inc("vv")
                em("vector", lambda e, c=oc:
                   e.tensor_tensor(out=xf[:, c * TG:(c + 1) * TG],
                                   in0=xf[:, c * TG:(c + 1) * TG],
                                   in1=stg[:, :], op=OP.add)
                   .then_inc(sems["vv"], 1))
                SBFREE["stg"] = ("vv", av)
                PSFREE[pnm] = ("vv", av)
            ln_on(xf, NC, 128, D)
            if amax:
                av = inc("vv")
                em("vector", lambda e, t=tg:
                   e.reduce_max(amax_cols[:, t:t + 1], xf[:, 0:NC * TG],
                                axis=AX, apply_absolute_value=True)
                   .then_inc(sems["vv"], 1))
            wait("sync", "vv", cnt["vv"])
            dst = xdst.rearrange("(c p) t -> p c t", p=128)[
                :, :, tg * TG:(tg + 1) * TG]
            dma("sync", "dS", dst, cb_dst(xf, NC, TG))

    def ffn_l0(xsrc, xdst):
        KF = int(os.environ.get("KFFN", "9"))
        wait("gpsimd", CPV)
        wait("sync", CPV)
        KWJ = int(os.environ.get("KWJ", "16"))
        fst = [True]
        for j in range(KWJ):
            guard("gpsimd")
            v = dma("gpsimd", "dW", cb_dst(wA, 16, 128),
                    ap["w0f"][:, j * 128:(j + 1) * 128].rearrange(
                        "(c p) q -> p c q", p=128))
            wait("vector", v)
            wv = inc("vv", 2)
            first = (j == 0)
            wv_mid = wv - 1

            def wfold(e, f=first, mid=wv_mid):
                e.reduce_max(amaxw[:, 0:1], wA[:, 0:16 * 128], axis=AX,
                             apply_absolute_value=True)\
                    .then_inc(sems["vv"], 1)
                e.wait_ge(sems["vv"], mid)
                (e.tensor_copy(out=amaxp[:, 1:2], in_=amaxw[:, 0:1]) if f else
                 e.tensor_tensor(out=amaxp[:, 1:2], in0=amaxp[:, 1:2],
                                 in1=amaxw[:, 0:1], op=OP.max))\
                    .then_inc(sems["vv"], 1)
            em("vector", wfold)
            if j == 0:
                dbg_dump(wA[:, 0:512], [("vv", wv)])
        wv2max = cnt["vv"]
        if KF < 2:
            return
        av = inc("vv")
        em("vector", lambda e: e.reduce_max(amaxp[:, 0:1], amax_cols[:, 0:4],
                                            axis=AX).then_inc(sems["vv"], 1))
        wait("scalar", "vv", av)
        gv2 = dma("scalar", "dA", arow[0:1, 0:128], amaxp[:, 0:1])
        wait("gpsimd", gv2)
        cv = dma("gpsimd", "dG", ccm_i[0:1, 0:128], arow[0:1, 0:128])
        wait("gpsimd", cv)
        CCV[0] += 1
        em("gpsimd", lambda e: e.collective_compute(
            "AllReduce", OP.max, replica_groups=[list(range(8))],
            ins=[ccm_i[:, :]], outs=[ccm_o[:, :]]).then_inc(sems["cc"], 1))
        wait("scalar", "cc", CCV[0])
        cv2 = dma("scalar", "dA", arow[0:1, 0:128], ccm_o[0:1, 0:128])
        wait("scalar", "vv", wv2max)
        gvw = dma("scalar", "dA", wrow[0:1, 0:128], amaxp[:, 1:2])
        tb_x = mm_group("pC", [ones1[0:1, :]], [arow[0:1, 0:512]],
                        deps=[cv2], width=512)
        tb_w = mm_group("pD", [ones1[0:1, :]], [wrow[0:1, 0:512]],
                        deps=[gvw], width=512)
        if KF < 3:
            return
        wait("vector", "tt", tb_w)
        wait("vector", "tt", tb_x)
        vbs = cnt["vv"]
        cnt["vv"] = vbs + 10
        sv0 = cnt["vv"]

        def scales(e):
            k = [vbs]

            def stp(i):
                k[0] += 1
                i.then_inc(sems["vv"], 1)
                e.wait_ge(sems["vv"], k[0])
            stp(e.reduce_max(sca[:, 0:1], psC[:, 0:128], axis=AX))
            stp(e.reduce_max(sca[:, 1:2], psD[:, 0:128], axis=AX))
            stp(e.tensor_scalar(out=sca[:, 0:1], in0=sca[:, 0:1],
                                scalar1=1.0 / 448.0, scalar2=1e-12,
                                op0=OP.mult, op1=OP.max))
            stp(e.tensor_scalar(out=sca[:, 1:2], in0=sca[:, 1:2],
                                scalar1=1.0 / 448.0, scalar2=1e-12,
                                op0=OP.mult, op1=OP.max))
            stp(e.tensor_tensor(out=sca[:, 2:3], in0=sca[:, 0:1],
                                in1=sca[:, 1:2], op=OP.mult))
            stp(e.tensor_scalar(out=sca[:, 2:3], in0=sca[:, 2:3],
                                scalar1=4.0, scalar2=None, op0=OP.mult))
            stp(e.tensor_scalar(out=sca[:, 0:1], in0=sca[:, 0:1],
                                scalar1=2.0, scalar2=None, op0=OP.mult))
            stp(e.tensor_scalar(out=sca[:, 1:2], in0=sca[:, 1:2],
                                scalar1=2.0, scalar2=None, op0=OP.mult))
            stp(e.reciprocal(out=sca[:, 0:1], in_=sca[:, 0:1]))
            e.reciprocal(out=sca[:, 1:2], in_=sca[:, 1:2])\
                .then_inc(sems["vv"], 1)
        em("vector", scales)
        PSFREE["pC"] = ("vv", sv0)
        PSFREE["pD"] = ("vv", sv0)
        wait("scalar", "vv", sv0)
        sv1 = inc("ss")
        em("scalar", lambda e: e.activation(out=sca[:, 3:4], in_=sca[:, 2:3],
                                            func=ACT.Copy)
           .then_inc(sems["ss"], 1))
        for tg in range(NG):
            if KF < 4 + tg:
                return
            xv, _ = emit_xload(xsrc, tg)
            wait("scalar", xv)
            wait("scalar", "ss", sv1)
            wait("scalar", "tt", cnt["tt"])
            xqv = inc("ss")
            em("scalar", lambda e:
               e.activation(out=x8[:, 0:NC * TG], in_=xf[:, 0:NC * TG],
                            func=ACT.Identity, scale=sca[:, 0:1])
               .then_inc(sems["ss"], 1))
            if tg == 0:
                dbg_dump(x8[:, 0:512], [("ss", cnt["ss"])])
            for oc in range(NC):
                wbuf = wA if oc % 2 == 0 else wB
                w8buf = w8a if oc % 2 == 0 else w8b
                guard("gpsimd")
                wv = dma("gpsimd", "dW", cb_dst(wbuf, 16, 128),
                         ap["w0f"][:, oc * 128:(oc + 1) * 128].rearrange(
                             "(c p) q -> p c q", p=128))
                wait("scalar", wv)
                wait("scalar", "tt", cnt["tt"])
                qv = inc("ss")
                em("scalar", lambda e, i=wbuf, o=w8buf:
                   e.activation(out=o[:, 0:16 * 128], in_=i[:, 0:16 * 128],
                                func=ACT.Identity, scale=sca[:, 1:2])
                   .then_inc(sems["ss"], 1))
                pnm = ["pA", "pB"][oc % 2]
                lh = [w8buf[:, k * 128:(k + 1) * 128] for k in range(16)]
                rh = [x8[:, k * TG:(k + 1) * TG] for k in range(16)]
                tv = mm_group(pnm, lh, rh, deps=[("ss", max(qv, xqv))])
                sv = evac(pnm, evq[:, :], tv, scaleap=sca[:, 2:3],
                          deps=sbdep("evq"))
                if tg == 0 and oc == 0:
                    dbg_dump(evq[:, 0:512], [("ss", sv)])
                wait("vector", "ss", sv)
                av = inc("vv")
                em("vector", lambda e, c=oc:
                   e.tensor_tensor(out=xf[:, c * TG:(c + 1) * TG],
                                   in0=xf[:, c * TG:(c + 1) * TG],
                                   in1=evq[:, :], op=OP.add)
                   .then_inc(sems["vv"], 1))
                SBFREE["evq"] = ("vv", av)
                PSFREE[pnm] = ("vv", av)
            ln_on(xf, NC, 128, D)
            wait("sync", "vv", cnt["vv"])
            dst = xdst.rearrange("(c p) t -> p c t", p=128)[
                :, :, tg * TG:(tg + 1) * TG]
            dma("sync", "dS", dst, cb_dst(xf, NC, TG))

    def ffn_l1(xsrc, xdst):
        for tg in range(NG):
            xv, xbv = emit_xload(xsrc, tg, cast=True)
            for fc in range(NHC):
                wbuf = wA if fc % 2 == 0 else wB
                guard("sync")
                wv = dma("sync", "dW", cb_dst(wbuf, 16, 128),
                         wgB[:, fc * 128:(fc + 1) * 128].rearrange(
                             "(c p) q -> p c q", p=128))
                wv2 = dma("sync", "dW", cb_dst(wC, 16, 128),
                          wuB[:, fc * 128:(fc + 1) * 128].rearrange(
                              "(c p) q -> p c q", p=128))
                pg = ["pA", "pB"][fc % 2]
                pu = ["pC", "pD"][fc % 2]
                lh = [wbuf[:, k * 128:(k + 1) * 128] for k in range(16)]
                rh = [xb[:, k * TG:(k + 1) * TG] for k in range(16)]
                tvg = mm_group(pg, lh, rh, deps=[wv, ("vv", xbv)])
                lh2 = [wC[:, k * 128:(k + 1) * 128] for k in range(16)]
                tvu = mm_group(pu, lh2, rh, deps=[wv2])
                sv = evac(pg, gg[:, :], tvg, func=ACT.Silu,
                          scaleap=sgc[:, fc:fc + 1], deps=sbdep("gg"))
                wait("vector", "ss", sv)
                wait("vector", "tt", tvu)
                hv = inc("vv")
                em("vector", lambda e, o=hh[:, fc * TG:(fc + 1) * TG],
                   i=PS[pu][:, 0:512], s=suc[:, fc:fc + 1]:
                   e.scalar_tensor_tensor(out=o, in0=i, scalar=s, in1=gg[:, :],
                                          op0=OP.mult, op1=OP.mult)
                   .then_inc(sems["vv"], 1))
                SBFREE["gg"] = ("vv", hv)
                PSFREE[pu] = ("vv", hv)
            hhv = cnt["vv"]
            for oc in range(NC):
                guard("sync")
                wv = dma("sync", "dW", cb_dst(wC, NHC, 128),
                         wdB[:, oc * 128:(oc + 1) * 128].rearrange(
                             "(c p) q -> p c q", p=128))
                pnm = ["pA", "pB"][oc % 2]
                lh = [wC[:, k * 128:(k + 1) * 128] for k in range(NHC)]
                rh = [hh[:, k * TG:(k + 1) * TG] for k in range(NHC)]
                tv = mm_group(pnm, lh, rh, deps=[wv, ("vv", hhv)])
                sv = evac(pnm, stg[:, :], tv,
                          scaleap=bias[:, bcol["sd"] + oc:bcol["sd"] + oc + 1],
                          deps=sbdep("stg"))
                wait("vector", "ss", sv)
                av = inc("vv")
                em("vector", lambda e, c=oc:
                   e.tensor_tensor(out=xf[:, c * TG:(c + 1) * TG],
                                   in0=xf[:, c * TG:(c + 1) * TG],
                                   in1=stg[:, :], op=OP.add)
                   .then_inc(sems["vv"], 1))
                SBFREE["stg"] = ("vv", av)
                PSFREE[pnm] = ("vv", av)
            ln_on(xf, NC, 128, D)
            wait("sync", "vv", cnt["vv"])
            dst = xdst.rearrange("(c p) t -> p c t", p=128)[
                :, :, tg * TG:(tg + 1) * TG]
            dma("sync", "dS", dst, cb_dst(xf, NC, TG))

    wait("vector", PRO)
    phases = [
        lambda: qkv_l0(ap["xT"]),
        lambda: (barrier(), allgather_kv(0), attention(0)),
        lambda: (barrier(), outproj_residual(0, "woT0", "bo0", ap["xT"],
                                             xbuf[0], amax=True)),
        lambda: (barrier(), ffn_l0(xbuf[0], xbuf[1])),
        lambda: (barrier(), qkv_l1(xbuf[1])),
        lambda: (barrier(), allgather_kv(1), attention(1)),
        lambda: (barrier(), outproj_residual(1, "wo1", "bo1", xbuf[1],
                                             xbuf[0], amax=False)),
        lambda: (barrier(), ffn_l1(xbuf[0], xout)),
    ]
    nph = int(os.environ.get("KPH", "8"))
    for ph in phases[:nph]:
        ph()
    if _DBG:
        barrier()
        dma("sync", "dS", dbg2T[0:1, 0:16], srow[0:1, :])
        dma("sync", "dS", dbg2T[0:128, 16:20], sca[:, :])
        dma("sync", "dS", dbg2T[0:1, 20:148], arow[0:1, :])
        dma("sync", "dS", dbg2T[0:128, 148:150], amaxp[:, :])
        dma("sync", "dS", dbg2T[0:128, 150:158], amax_cols[:, :])
        dma("sync", "dS", dbg2T[0:1, 160:168], arow[0:1, 0:8])
        dma("sync", "dS", dbg2T[0:1, 168:176], wrow[0:1, 0:8])
        dma("sync", "dS", dbg2T[0:128, 176:180], bc2[:, 0:4])
        dma("sync", "dS", dbg2T[0:128, 180:184], bc2[:, 512:516])
        dma("sync", "dS", dbg2T[0:128, 184:186], amaxp[:, :])
        dma("sync", "dS", dbg2T[0:128, 186:190], wA[:, 0:4])
        dma("sync", "dS", dbg2T[0:128, 190:192], amaxw[:, 0:1])
        dma("sync", "dS", dbg2T[0:2, 192:196], ap["w0f"][0:2, 0:4])
        dma("sync", "dS", dbg2T[0:2, 196:200],
            ap["woT0"][0:2, 0:4])
        dma("sync", "dS", dbg2T[0:2, 200:204],
            ap["wgT"][0:2, 0:4].bitcast(F32))
        dma("sync", "dS", dbg2T[0:2, 208:212], w0Ti[0:2, 0:4])
    for e in prog:
        wait(e, "dS", cnt["dS"])
        wait(e, "dA", cnt["dA"])
        wait(e, "dG", cnt["dG"])

    for eng_name in ["sync", "vector", "scalar", "tensor", "gpsimd"]:
        thunks = prog[eng_name]

        def make(fns):
            def body(e):
                for f in fns:
                    f(e)
            return body
        getattr(block, eng_name)(make(thunks))

    st.close()
    return nc


def kernel(**inputs):
    in_maps = stage_inputs(inputs)
    nc = build_program()
    res = run_bass_kernel_spmd(nc, in_maps, core_ids=list(range(8)))
    out = np.zeros((B, T, D), np.float32)
    for c in range(8):
        b, p = c // 2, c % 2
        xo = np.asarray(res.results[c]["xout"])
        for i in range(NQT):
            gq = 2 * i + p
            out[b, 128 * gq:128 * gq + 128] = xo[:, 128 * i:128 * i + 128].T
    return out
